# version 01 (final)
# baseline (speedup 1.0000x reference)
import numpy as np
import contextlib
import concourse.bass as bass
import concourse.mybir as mybir
from concourse.bass_utils import run_bass_kernel_spmd
import ml_dtypes

F32 = mybir.dt.float32
BF16 = mybir.dt.bfloat16
U32 = mybir.dt.uint32
AF = mybir.ActivationFunctionType
ALU = mybir.AluOpType
AX = mybir.AxisListType
bfd = ml_dtypes.bfloat16

S, OWN, D = 4096, 2048, 4096
HQ, KVH, DH = 32, 8, 128
CB, NSEL, WIN = 16, 4, 64
NBLK = S // CB
NQT = OWN // 128
SC = float(DH) ** -0.5
EPS = 1e-5
ENGINES = ("sync", "tensor", "vector", "scalar", "gpsimd")


class Chan:
    __slots__ = ("name", "step", "n", "h")

    def __init__(self, name, step):
        self.name, self.step, self.n, self.h = name, step, 0, None


class Prog:
    def __init__(self, nc, ctx):
        self.nc, self.ctx = nc, ctx
        self.ops = {e: [] for e in ENGINES}
        self.chans = []
        self.frozen = False

    def chan(self, name, step=1):
        ch = Chan(name, step)
        ch.h = self.ctx.enter_context(self.nc.semaphore(name))
        self.chans.append(ch)
        return ch

    def op(self, eng, fn, waits=(), incs=()):
        if self.frozen:
            return
        winfo = [(ch, v) for ch, v in waits if v > 0]
        iinfo = []
        for it in incs:
            ch, k = it if isinstance(it, tuple) else (it, 1)
            ch.n += ch.step * k
            iinfo.append((ch, ch.step * k))
        self.ops[eng].append((fn, winfo, iinfo))

    def emit(self, block):
        for eng in ENGINES:
            ops = self.ops[eng]
            if not ops:
                continue

            def section(e, ops=ops):
                for fn, winfo, iinfo in ops:
                    for ch, v in winfo:
                        e.wait_ge(ch.h, v)
                    instr = fn(e)
                    if iinfo:
                        if instr is not None:
                            for ch, k in iinfo:
                                instr.then_inc(ch.h, k)
                        else:
                            for ch, k in iinfo:
                                e.sem_inc(ch.h, k)

            getattr(block, eng)(section)


def _host_tables_shared():
    t = {}
    t["tb_ident"] = np.eye(128, dtype=np.float32).astype(bfd)
    sw = np.zeros((128, 128), dtype=np.float32)
    for i in range(64):
        sw[2 * i + 1, 2 * i] = -1.0
        sw[2 * i, 2 * i + 1] = 1.0
    t["tb_swap"] = sw.astype(bfd)
    t["tb_ones1"] = np.ones((128, 1), dtype=np.float32).astype(bfd)
    t["tb_iota28"] = np.broadcast_to(np.arange(28, dtype=np.float32), (128, 28)).copy()
    perm = np.zeros((128, 4, 128), dtype=np.float32)
    permg = np.zeros((128, 4, 128), dtype=np.float32)
    for gi in range(4):
        for p in range(128):
            perm[gi * 32 + p // 4, gi, p] = 1.0
            permg[gi * 32 + (p % 32), gi, p] = 1.0
    t["tb_perm"] = perm.reshape(128, 512).astype(bfd)
    t["tb_permg"] = permg.reshape(128, 512).astype(bfd)
    m48 = np.zeros((128, 32), dtype=np.float32)
    mg4 = np.zeros((128, 16), dtype=np.float32)
    for p in range(128):
        for gi in range(4):
            m48[p, gi * 8 + p % 4] = 1.0
            m48[p, gi * 8 + 4 + p % 4] = 1.0
            mg4[p, gi * 4 + p // 32] = 1.0
    t["tb_m48"] = m48
    t["tb_mg4"] = mg4
    mg4b = np.zeros((128, 32), dtype=np.float32)
    for p in range(128):
        for gi in range(4):
            mg4b[p, gi * 8 + p // 32] = 1.0
    t["tb_mg4b"] = mg4b
    mo = np.zeros((128, 128), dtype=np.float32)
    for p in range(128):
        for g in range(4):
            mo[p, g * 32 + p // 4] = 1.0
    t["tb_mown16"] = np.tile(mo, (1, 16)).astype(bfd)
    md = np.zeros((32, 128), dtype=np.float32)
    for tt in range(32):
        for g in range(4):
            for qi in range(32):
                if tt // 16 == qi // 16 and tt % 16 <= qi % 16:
                    md[tt, g * 32 + qi] = 1.0
    t["tb_mdiag"] = md.astype(bfd)
    cm2 = np.zeros((8, 128), dtype=np.float32)
    cm2[0, :] = 1.0
    t["tb_cm2"] = cm2.astype(bfd)
    smb = np.zeros((64, 128), dtype=np.float32)
    for tt in range(64):
        smb[tt, 64 + tt:] = 1.0
    t["tb_smb"] = smb.astype(bfd)
    inv = 1.0 / 10000.0 ** (np.arange(0, DH, 2, dtype=np.float32) / DH)
    ang = np.arange(S, dtype=np.float32)[None, :] * inv[:, None]
    t["tb_cosk"] = np.repeat(np.cos(ang).astype(np.float32), 2, axis=0)
    t["tb_sink"] = np.repeat(np.sin(ang).astype(np.float32), 2, axis=0)
    return t


_HT = _host_tables_shared()


def _host_tables_core(core):
    own0 = (core % 2) * OWN
    t = {}
    t["tb_cosq"] = np.ascontiguousarray(_HT["tb_cosk"][:, own0:own0 + OWN])
    t["tb_sinq"] = np.ascontiguousarray(_HT["tb_sink"][:, own0:own0 + OWN])
    cm1 = np.zeros((NQT, 128, 256), dtype=np.float32)
    for qt in range(NQT):
        for qq in range(128):
            nvis = (own0 + qt * 128 + qq) // 16
            for cc in range(2):
                n = min(max(nvis - cc * 128, 0), 128)
                if n > 0:
                    cm1[qt, 0:n, cc * 128 + qq] = 1.0
    t["tb_cm1"] = cm1.astype(bfd)
    sma = np.zeros((NQT, 128, 128), dtype=np.float32)
    for qt in range(NQT):
        for tt in range(128):
            ka = own0 + 128 * qt - 64 + tt
            if ka < 0:
                continue
            for qq in range(128):
                rel = (own0 + 128 * qt + qq) - ka
                if 0 <= rel <= WIN:
                    sma[qt, tt, qq] = 1.0
    t["tb_sma"] = sma.astype(bfd)
    swa = np.zeros((NQT, 128, 1), dtype=np.uint32)
    swb = np.zeros((NQT, 128, 1), dtype=np.uint32)
    for qt in range(NQT):
        for tt in range(128):
            swa[qt, tt, 0] = max(own0 + 128 * qt - 64 + tt, 0)
            swb[qt, tt, 0] = min(own0 + 128 * qt + 64 + tt, S - 1)
    t["tb_swa"] = swa
    t["tb_swb"] = swb
    lsc = np.zeros((128, 32), dtype=np.uint32)
    for tt in range(32):
        for p in range(128):
            loc = 128 * tt + p - own0
            lsc[p, tt] = loc if 0 <= loc < OWN else (2 * OWN + 128 * tt + p)
    t["tb_lnsc"] = lsc
    return t


def build(dbg=(), phases=("A", "B", "C", "D"), d_qtiles=NQT, comps="csf", races=True, pad=(0, 0), dlim=None):
    nc = bass.Bass("TRN2", target_bir_lowering=False, detect_race_conditions=races)
    dram = {}

    def din(name, shape, dt):
        dram[name] = nc.dram_tensor(name, list(shape), dt, kind="ExternalInput").ap()
        return dram[name]

    def dint(name, shape, dt):
        kind = "ExternalOutput" if name in dbg else "Internal"
        dram[name] = nc.dram_tensor(name, list(shape), dt, kind=kind).ap()
        return dram[name]

    x_full = din("x_full", (S, D), F32)
    x_own = din("x_own", (OWN, D), F32)
    for nm, sh in [("wqkvT", (D, 6144)), ("w_oT", (D, D)), ("bWT", (D, D)),
                   ("k_w1T", (2048, 2048)), ("v_w1T", (2048, 2048)),
                   ("k_w2T", (2048, 128)), ("v_w2T", (2048, 128)),
                   ("wgT", (D, 96)), ("lgAT", (D, 252)), ("bflat", (224, D)),
                   ("kb1", (16, 128, 1)), ("vb1", (16, 128, 1)), ("kb2", (128, 1)),
                   ("vb2bc", (128, 128)), ("bgbc", (128, 96)),
                   ("kposT", (KVH, 128, 16)), ("vposT", (KVH, 128, 16)),
                   ("memk", (KVH, 128, 1)), ("memv", (KVH, 1, 128))]:
        din(nm, sh, F32)
    din("tb_identF", (128, 128), F32)
    for nm, arr in {**_HT, **_host_tables_core(0)}.items():
        dt = U32 if arr.dtype == np.uint32 else (BF16 if arr.dtype == bfd else F32)
        din(nm, arr.shape, dt)

    out_d = nc.dram_tensor("out", [OWN, D], F32, kind="ExternalOutput").ap()

    ln_d = dint("ln_d", (S, D), BF16)
    lnow_d = dint("lnow_d", (3 * OWN + S, D), BF16)
    kblk_d = dint("kblk_d", (KVH, S, DH), BF16)
    rkblk_d = dint("rkblk_d", (KVH, S, DH), BF16)
    vblk_d = dint("vblk_d", (KVH, S, DH), BF16)
    q_d = dint("q_d", (NQT, 128, HQ * 128), BF16)
    rq_d = dint("rq_d", (NQT, 128, HQ * 128), BF16)
    gates_d = dint("gates_d", (OWN, 96), F32)
    ln2_d = dint("ln2_d", (OWN, D), BF16)
    mid_d = dint("mid_d", (OWN, D), F32)
    dbgsel_d = dint("dbgsel", (NQT, 128, 8 * 8), F32) if "dbgsel" in dbg else None
    dbglnt_d = dint("dbglnt", (32, 128, 1024), BF16) if "dbgx" in dbg else None
    dbgck_d = dint("dbgck", (128, KVH * 264), BF16) if "dbgck" in dbg else None
    dbgacc_d = dint("dbgacc", (3, 128, HQ * 128), F32) if "dbgacc" in dbg else None
    dbgwin_d = dint("dbgwin", (4, 128, 192), BF16) if "dbgwin" in dbg else None
    dbgfin_d = dint("dbgfin", (128, 4096), F32) if "dbgfin" in dbg else None
    dbgfin2_d = dint("dbgfin2", (2, 128, 2048), BF16) if "dbgfin" in dbg else None
    dbgidx_d = dint("dbgidx", (128, 4), U32) if "dbgfin" in dbg else None
    if "dbginit" in dbg:
        dint("dbgcv0i", (128, KVH * 129), BF16)
    dbgktp_d = dint("dbgktp", (2, 128, 8192), BF16) if "dbgktp" in dbg else None
    dbgcv_d = dint("dbgcv", (3, 128, KVH * 129), BF16) if "dbgck" in dbg else None
    if "freeze1" in dbg:
        dint("dbgwq", (2, 128, 512), BF16)
    dbgev_d = dint("dbgev", (4, 128, 512), BF16) if "dbgx" in dbg else None
    stg = {}
    for nm, sh in [("wqkvT_bf", (D, 6144)), ("w_oT_bf", (D, D)), ("bWT_bf", (D, D)),
                   ("k_w1T_bf", (2048, 2048)), ("v_w1T_bf", (2048, 2048)),
                   ("k_w2T_bf", (2048, 128)), ("v_w2T_bf", (2048, 128)),
                   ("wgT_bf", (D, 96)), ("lgAT_bf", (D, 252)), ("bflat_bf", (224, D))]:
        stg[nm] = dint(nm, sh, BF16)

    ctx = contextlib.ExitStack()
    nc._kernel_ctx = ctx
    sbt = lambda n, sh, dt: ctx.enter_context(nc.sbuf_tensor(n, sh, dt))
    pst = lambda n, sh, dt: ctx.enter_context(nc.psum_tensor(n, sh, dt))

    P = [pst(f"psum{i}", [128, 512], F32) for i in range(7)]
    PTb = pst("psumT", [128, 512], BF16)

    ident = sbt("ident", [128, 128], BF16)
    identF = sbt("identF", [128, 128], F32)
    swap_s = sbt("swap_s", [128, 128], BF16)
    ones1 = sbt("ones1", [128, 1], BF16)
    iota28 = sbt("iota28", [128, 28], F32)
    perm_s = sbt("perm_s", [128, 512], BF16)
    permg_s = sbt("permg_s", [128, 512], BF16)
    m48_s = sbt("m48_s", [128, 32], F32)
    mg4_s = sbt("mg4_s", [128, 16], F32)
    mg4b_s = sbt("mg4b_s", [128, 32], F32)
    gsel = sbt("gsel", [128, 8], BF16)
    idxf = sbt("idxf", [128, 4], F32)
    mown_s = sbt("mown_s", [128, 2048], BF16)
    mdiag_s = sbt("mdiag_s", [32, 128], BF16)
    cm2_s = sbt("cm2_s", [8, 128], BF16)
    smb_s = sbt("smb_s", [64, 128], BF16)
    kposT_s = sbt("kposT_s", [128, KVH * 16], F32)
    vposT_s = sbt("vposT_s", [128, KVH * 16], F32)
    kb1_s = sbt("kb1_s", [128, 16], F32)
    vb1_s = sbt("vb1_s", [128, 16], F32)
    kb2_s = sbt("kb2_s", [128, 1], F32)
    vb2bc_s = sbt("vb2bc_s", [128, 128], F32)
    bgbc_s = sbt("bgbc_s", [128, 96], F32)
    memk_s = sbt("memk_s", [128, KVH], F32)
    lnsc_s = sbt("lnsc_s", [128, 32], U32)
    swa_s = sbt("swa_s", [128, NQT], U32)
    swb_s = sbt("swb_s", [128, NQT], U32)
    wg_s = sbt("wg_s", [128, 32 * 96], BF16)
    ckT = sbt("ckT", [128, KVH * 264], BF16)
    cv0 = sbt("cv0", [128, KVH * 129], BF16)
    cv1 = sbt("cv1", [128, KVH * 129], BF16)
    cv2 = sbt("cv2", [8, KVH * 129], BF16)
    acc = sbt("acc", [128, HQ * 128], F32)
    vwA = sbt("vwA", [128, 129], BF16)
    vwB = sbt("vwB", [128, 129], BF16)
    rkwA = sbt("rkwA", [128, 128], BF16)
    rkwB = sbt("rkwB", [128, 128], BF16)
    bn48 = sbt("bn48", [128, 4 * 48], F32)
    mv2 = sbt("mv2", [128, 2], F32)
    stdv = sbt("stdv", [128, 1], F32)
    rstd = sbt("rstd", [128, 1], F32)
    zimp = sbt("zimp", [128, 1], F32)
    timp = sbt("timp", [128, 1], F32)
    top8 = sbt("top8", [128, 8], F32)
    mi8 = sbt("mi8", [128, 8], U32)
    selfsel = sbt("selfsel", [128, 8], BF16)
    selred = sbt("selred", [128, 8], F32)
    idxcols = sbt("idxcols", [128, 4], U32)
    fselcols = sbt("fselcols", [128, 4], F32)
    gpred = sbt("gpred", [128, 4], F32)
    rz = sbt("rz", [128, 1], F32)
    wcol = sbt("wcol", [128, 1], F32)
    gat = sbt("gat", [128, 96], F32)
    gatb = sbt("gatb", [128, 96], BF16)
    est_c2 = sbt("est_c2", [8, 128], BF16)
    est_sB = sbt("est_sB", [64, 128], BF16)
    est_fo = sbt("est_fo", [32, 128], BF16)
    vown_s = sbt("vown_s", [32, 129], BF16)
    pt_a = sbt("pt_a", [128, 128], F32)
    pt_b = sbt("pt_b", [128, 512], F32)
    gfull = sbt("gfull", [128, 28], F32)
    gf2 = sbt("gf2", [128, 28], F32)
    gexp = sbt("gexp", [128, 224], F32)
    hgq = sbt("hgq", [128, 224], BF16)
    p1c = sbt("p1c", [128, 1], F32)
    p2c = sbt("p2c", [128, 1], F32)
    i0f = sbt("i0f", [128, 1], F32)
    i1f = sbt("i1f", [128, 1], F32)
    vdif = sbt("vdif", [128, 1], F32)

    AB = sbt("arenaB", [128, 53248], BF16)
    AFr = sbt("arenaF", [128, 12288], F32)
    cbv = lambda st, cols: AB[:, st:st + cols]
    cfv = lambda st, cols: AFr[:, st:st + cols]

    Pg = Prog(nc, ctx)
    c_ld = Pg.chan("c_ld", 16)
    c_wc2 = [Pg.chan("c_wcA", 16), Pg.chan("c_wcB", 16)]
    c_wk = Pg.chan("c_wk", 1)
    c_ln = Pg.chan("c_ln", 16)
    c_lnk = Pg.chan("c_lnk", 1)
    c_sc = Pg.chan("c_sc", 16)
    c_sck = Pg.chan("c_sck", 1)
    c_lt = Pg.chan("c_lt", 16)
    c_wl2 = [Pg.chan("c_wlA", 16), Pg.chan("c_wlB", 16)]
    c_bf = Pg.chan("c_bf", 16)
    c_w2 = Pg.chan("c_w2", 16)
    c_bk = Pg.chan("c_bk", 16)
    c_bq = Pg.chan("c_bq", 16)
    c_brv = Pg.chan("c_brv", 16)
    c_gw = Pg.chan("c_gw", 16)
    c_gf2 = [Pg.chan("c_gfA", 16), Pg.chan("c_gfB", 16)]
    c_st = Pg.chan("c_st", 16)
    c_mt2 = [Pg.chan("c_mtA", 16), Pg.chan("c_mtB", 16)]
    c_l2 = Pg.chan("c_l2", 16)
    c_x2 = [Pg.chan("c_xA", 16), Pg.chan("c_xB", 16)]
    c_xm = Pg.chan("c_xm", 16)
    c_pe = Pg.chan("c_pe", 1)
    c_ve = Pg.chan("c_ve", 1)
    c_se = Pg.chan("c_se", 1)
    c_ot2 = [Pg.chan("c_otA", 16), Pg.chan("c_otB", 16)]
    OP = Pg.op

    for _ in range(pad[0]):
        OP("vector", lambda e: e.nop())
    for _ in range(pad[1]):
        OP("scalar", lambda e: e.nop())
    for _ in range(pad[2] if len(pad) > 2 else 0):
        OP("tensor", lambda e: e.nop())
    for _ in range(pad[3] if len(pad) > 3 else 0):
        OP("tensor", lambda e: e.matmul(P[0][:, 0:128], ident[:], ident[:, 0:128],
                                        start=True, stop=True))

    OP("vector", lambda e: e.memset(gsel[:], 0.0), incs=[c_ve])
    for dst, nm in [(ident, "tb_ident"), (identF, "tb_identF"), (swap_s, "tb_swap"), (ones1, "tb_ones1"),
                    (iota28, "tb_iota28"), (m48_s, "tb_m48"), (mg4_s, "tb_mg4"), (mg4b_s, "tb_mg4b"),
                    (mown_s, "tb_mown16"), (mdiag_s, "tb_mdiag"), (cm2_s, "tb_cm2"),
                    (smb_s, "tb_smb"), (kb2_s, "kb2"), (vb2bc_s, "vb2bc"),
                    (bgbc_s, "bgbc"), (perm_s, "tb_perm"), (permg_s, "tb_permg"),
                    (lnsc_s, "tb_lnsc")]:
        OP("sync", lambda e, d=dst, n=nm: e.dma_start(out=d[:], in_=dram[n][:]), incs=[c_ld])
    for dst, nm, hh in [(kposT_s, "kposT", KVH), (vposT_s, "vposT", KVH),
                        (kb1_s, "kb1", 16), (vb1_s, "vb1", 16), (memk_s, "memk", KVH)]:
        OP("sync", lambda e, d=dst, n=nm, hh=hh: e.dma_start(
            out=d[:].rearrange("p (h c) -> p h c", h=hh),
            in_=dram[n].rearrange("h a b -> a h b"),
            allow_slow_non_contiguous=True), incs=[c_ld])
    OP("sync", lambda e: e.dma_start(out=swa_s[:].rearrange("p (t c) -> p t c", t=NQT),
                                     in_=dram["tb_swa"].rearrange("t a b -> a t b"),
                                     allow_slow_non_contiguous=True), incs=[c_ld])
    OP("sync", lambda e: e.dma_start(out=swb_s[:].rearrange("p (t c) -> p t c", t=NQT),
                                     in_=dram["tb_swb"].rearrange("t a b -> a t b"),
                                     allow_slow_non_contiguous=True), incs=[c_ld])
    LD_ALL = c_ld.n

    OP("vector", lambda e: e.memset(cv0[:].rearrange("p (h c) -> p h c", h=KVH)[:, :, 128:129], 1.0),
       waits=[(c_ld, LD_ALL)], incs=[c_ve])
    OP("vector", lambda e: e.memset(cv1[:].rearrange("p (h c) -> p h c", h=KVH)[:, :, 128:129], 1.0), incs=[c_ve])
    OP("vector", lambda e: e.memset(cv2[:], 0.0), incs=[c_ve])
    OP("vector", lambda e: e.memset(cv2[:].rearrange("p (h c) -> p h c", h=KVH)[:, :, 128:129], 1.0),
       waits=[(c_ve, c_ve.n)], incs=[c_ve])
    OP("vector", lambda e: e.memset(vwA[:, 128:129], 1.0), incs=[c_ve])
    OP("vector", lambda e: e.memset(vwB[:, 128:129], 1.0), incs=[c_ve])
    OP("vector", lambda e: e.memset(acc[:], 0.0), incs=[c_ve])
    V_INIT = c_ve.n
    if "dbginit" in dbg:
        OP("scalar", lambda e: e.dma_start(out=dram["dbgcv0i"][:], in_=cv0[:]),
           waits=[(c_ve, V_INIT)], incs=[c_st])

    wdone = {}
    castb = [cbv(49152, 2048), cbv(51200, 2048)]
    cast_last = [0, 0]
    kc = 0
    for nm, R, Cc in [("wqkvT", D, 6144), ("wgT", D, 96), ("w_oT", D, D), ("bWT", D, D),
                      ("k_w1T", 2048, 2048), ("v_w1T", 2048, 2048), ("k_w2T", 2048, 128),
                      ("v_w2T", 2048, 128), ("lgAT", D, 252), ("bflat", 224, D)]:
        src, dst = dram[nm], stg[nm + "_bf"]
        for r in range((R + 127) // 128):
            r0, r1 = r * 128, min(R, r * 128 + 128)
            cw = 0
            while cw < Cc:
                c1 = min(Cc, cw + 2048)
                bsel = kc % 2
                ch = c_wc2[bsel]
                buf = castb[bsel][0:r1 - r0, 0:c1 - cw]
                OP("gpsimd", lambda e, b=buf, s=src, a=r0, bb=r1, c0=cw, c2=c1:
                   e.dma_start(out=b, in_=s[a:bb, c0:c2]),
                   waits=[(ch, cast_last[bsel])], incs=[ch])
                OP("gpsimd", lambda e, b=buf, dd=dst, a=r0, bb=r1, c0=cw, c2=c1:
                   e.dma_start(out=dd[a:bb, c0:c2], in_=b),
                   waits=[(ch, ch.n)], incs=[ch])
                cast_last[bsel] = ch.n
                kc += 1
                cw = c1
        OP("gpsimd", lambda e: e.nop(),
           waits=[(c_wc2[0], c_wc2[0].n), (c_wc2[1], c_wc2[1].n)], incs=[c_wk])
        wdone[nm] = c_wk.n

    xa = [cfv(4096, 4096), cfv(8192, 4096)]
    lnt2b = [cbv(40960, 4096), cbv(45056, 4096)]
    xa_last = [0, 0]
    for tt in range(32):
        xt = xa[tt % 2]
        lt_ = lnt2b[tt % 2]
        OP("sync", lambda e, x=xt, a=tt: e.dma_start(out=x, in_=x_full[a * 128:(a + 1) * 128, :]),
           waits=[(c_ve, xa_last[tt % 2])], incs=[c_x2[tt % 2]])
        for j in range(8):
            OP("vector", lambda e, x=xt, j=j: e.bn_stats(out=bn48[:, 6 * j:6 * j + 6],
                                                         in_=x[:, 512 * j:512 * (j + 1)]),
               waits=[(c_x2[tt % 2], c_x2[tt % 2].n)] if j == 0 else [],
               incs=[c_ve] if j == 7 else [])
        OP("vector", lambda e: e.bn_aggr(out=mv2[:], in_=bn48[:, 0:48]),
           waits=[(c_ve, c_ve.n)], incs=[c_ve])
        OP("vector", lambda e: e.tensor_scalar(out=stdv[:], in0=mv2[:, 1:2], scalar1=EPS,
                                               scalar2=None, op0=ALU.add, op1=ALU.bypass),
           waits=[(c_ve, c_ve.n)], incs=[c_ve])
        OP("scalar", lambda e: e.activation(out=stdv[:], in_=stdv[:], func=AF.Sqrt),
           waits=[(c_ve, c_ve.n)], incs=[c_se])
        OP("vector", lambda e: e.reciprocal(out=rstd[:], in_=stdv[:]),
           waits=[(c_se, c_se.n)], incs=[c_ve])
        OP("vector", lambda e, x=xt, lt_=lt_, b=tt % 2: e.tensor_scalar(
            out=lt_, in0=x, scalar1=mv2[:, 0:1], scalar2=rstd[:],
            op0=ALU.subtract, op1=ALU.mult),
           waits=[(c_ve, c_ve.n), (c_lnk, max(0, tt - 1)), (c_sck, max(0, tt - 1))],
           incs=[c_ve])
        OP("scalar", lambda e, a=tt, lt_=lt_: e.dma_start(
            out=ln_d[a * 128:(a + 1) * 128, :], in_=lt_),
           waits=[(c_ve, c_ve.n)], incs=[c_ln])
        OP("gpsimd", lambda e, a=tt, lt_=lt_: e.indirect_dma_start(
            out=lnow_d[:], out_offset=bass.IndirectOffsetOnAxis(ap=lnsc_s[:, a:a + 1], axis=0),
            in_=lt_, in_offset=None, bounds_check=OWN - 1, oob_is_err=False),
           waits=[(c_ve, c_ve.n)], incs=[c_sc])
        OP("scalar", lambda e: e.nop(), waits=[(c_ln, c_ln.n)], incs=[c_lnk])
        OP("gpsimd", lambda e: e.nop(), waits=[(c_sc, c_sc.n)], incs=[c_sck])
        xa_last[tt % 2] = c_ve.n
    A_LN, A_SC = c_ln.n, c_sc.n

    if "B" in phases:
        lnT = [cbv(i * 1024, 1024) for i in range(32)]
        evq = [cbv(32768 + i * 512, 512) for i in range(4)]
        wq4 = [cbv(34816 + i * 512, 512) for i in range(2)]
        rkbf = cbv(35840, 512)
        tokm = [cbv(36352, 512), cbv(36864, 512)]
        gtmp = cfv(0, 96)
        ropf = [cfv(128, 512), cfv(640, 512)]
        ropec = cfv(1152, 1024)
        ropes = cfv(2176, 1024)
        evq_last = [0, 0, 0, 0]
        rkbf_last = [0]
        gat_last = [0]
        tok_last = [0, 0]
        wq_last = [0, 0]

        OP("sync", lambda e: e.dma_start(out=wg_s[:].rearrange("p (f c) -> p f c", f=32),
                                         in_=stg["wgT_bf"].rearrange("(f p) c -> p f c", p=128)),
           waits=[(c_wk, wdone["wgT"])], incs=[c_ld])
        WG_LD = c_ld.n

        def load_lnT(src_dram, row0, extra):
            for f in range(32):
                OP("sync", lambda e, f=f, src=src_dram, row0=row0: e.dma_start(
                    out=lnT[f], in_=src[row0:row0 + 1024, f * 128:(f + 1) * 128], transpose=True),
                   waits=([(c_pe, c_pe.n)] + extra) if f == 0 else [], incs=[c_lt])

        def tokstore(srcb, bsel, dst_ap):
            OP("scalar", lambda e, b=bsel: e.copy(out=tokm[b], in_=PTb[:]),
               waits=[(c_pe, c_pe.n), (c_bk, tok_last[bsel])], incs=[c_se])
            OP("scalar", lambda e, b=bsel, d=dst_ap: e.dma_start(
                out=d.rearrange("(c p) d -> p c d", p=128),
                in_=tokm[b].rearrange("p (c d) -> p c d", d=128)),
               waits=[(c_se, c_se.n)], incs=[c_bk])
            tok_last[bsel] = c_bk.n

        for tc in range(4):
            t0c = tc * 1024
            load_lnT(ln_d, t0c, [(c_lnk, 8 * tc + 8)])
            OP("sync", lambda e, t0c=t0c: e.dma_start(out=ropec, in_=dram["tb_cosk"][:, t0c:t0c + 1024]),
               incs=[c_lt])
            OP("sync", lambda e, t0c=t0c: e.dma_start(out=ropes, in_=dram["tb_sink"][:, t0c:t0c + 1024]),
               incs=[c_lt])
            LTN = c_lt.n
            if dbglnt_d is not None and tc == 0:
                for f in range(32):
                    OP("scalar", lambda e, f=f: e.dma_start(out=dbglnt_d[f], in_=lnT[f]),
                       waits=[(c_lt, LTN)] if f == 0 else [], incs=[c_st])
            for grp in range(4):
                ocs = [32 + grp * 4 + i for i in range(4)]
                for nh in range(2):
                    t0 = t0c + nh * 512
                    for f in range(32):
                        bsel = f % 2
                        wt = wq4[bsel]
                        OP("sync", lambda e, w=wt, f=f, o0=ocs[0]: e.dma_start(
                            out=w, in_=stg["wqkvT_bf"][f * 128:(f + 1) * 128, o0 * 128:o0 * 128 + 512]),
                           waits=[(c_pe, wq_last[bsel]),
                                  (c_wk, wdone["wqkvT"] if (tc == 0 and grp == 0 and nh == 0 and f < 2) else 0),
                                  (c_lt, LTN if f == 0 else 0)],
                           incs=[c_wl2[bsel]])
                        for oi in range(4):
                            OP("tensor", lambda e, f=f, oi=oi, nh=nh, w=wt: e.matmul(
                                P[oi][:], w[:, oi * 128:(oi + 1) * 128],
                                lnT[f][:, nh * 512:(nh + 1) * 512],
                                start=(f == 0), stop=(f == 31)),
                               waits=[(c_wl2[bsel], c_wl2[bsel].n),
                                      (c_se, c_se.n if f == 0 else 0)] if oi == 0 else [],
                               incs=[c_pe] if oi == 3 else [])
                        wq_last[bsel] = c_pe.n
                    MM_DONE = c_pe.n
                    for oi in range(4):
                        oc = ocs[oi]
                        is_k = oc < 40
                        kvh = oc - 32 if is_k else oc - 40
                        OP("scalar", lambda e, oi=oi: e.copy(out=evq[oi], in_=P[oi][:]),
                           waits=[(c_pe, MM_DONE), (c_ve, c_ve.n)], incs=[c_se])
                        EV_OK = c_se.n
                        if dbgev_d is not None and tc == 0 and grp == 0 and nh == 0:
                            OP("scalar", lambda e, oi=oi: e.dma_start(out=dbgev_d[oi], in_=evq[oi]),
                               waits=[(c_se, EV_OK)], incs=[c_st])
                            if oi == 3 and "freeze1" in dbg:
                                OP("scalar", lambda e: e.dma_start(out=dram["dbgwq"][0], in_=wq4[0]),
                                   incs=[c_st])
                                OP("scalar", lambda e: e.dma_start(out=dram["dbgwq"][1], in_=wq4[1]),
                                   incs=[c_st])
                                for ff in range(32):
                                    OP("scalar", lambda e, ff=ff: e.dma_start(
                                        out=dbglnt_d[ff], in_=lnT[ff]), incs=[c_st])
                                Pg.frozen = True
                        if is_k:
                            OP("tensor", lambda e, oi=oi: e.matmul(P[4][:], swap_s[:], evq[oi],
                                                                   start=True, stop=True),
                               waits=[(c_se, EV_OK), (c_ve, c_ve.n)], incs=[c_pe])
                            OP("vector", lambda e, nh=nh: e.tensor_tensor(
                                out=ropf[0], in0=P[4][:], in1=ropes[:, nh * 512:(nh + 1) * 512],
                                op=ALU.mult),
                               waits=[(c_pe, c_pe.n)], incs=[c_ve])
                            OP("vector", lambda e, oi=oi, nh=nh: e.tensor_tensor(
                                out=ropf[1], in0=evq[oi], in1=ropec[:, nh * 512:(nh + 1) * 512],
                                op=ALU.mult), incs=[c_ve])
                            OP("vector", lambda e: e.tensor_tensor(out=rkbf, in0=ropf[0], in1=ropf[1],
                                                                   op=ALU.add),
                               waits=[(c_ve, c_ve.n)], incs=[c_ve])
                            for q in range(4):
                                OP("tensor", lambda e, q=q: e.transpose(
                                    PTb[:, q * 128:(q + 1) * 128], rkbf[:, q * 128:(q + 1) * 128],
                                    ident[:]),
                                   waits=[(c_ve, c_ve.n), (c_se, c_se.n)] if q == 0 else [],
                                   incs=[c_pe] if q == 3 else [])
                            tokstore(rkbf, 0, rkblk_d[kvh][t0:t0 + 512, :])
                            for q in range(4):
                                OP("tensor", lambda e, q=q, oi=oi: e.transpose(
                                    PTb[:, q * 128:(q + 1) * 128], evq[oi][:, q * 128:(q + 1) * 128],
                                    ident[:]),
                                   waits=[(c_se, c_se.n)] if q == 0 else [],
                                   incs=[c_pe] if q == 3 else [])
                            tokstore(evq[oi], 1, kblk_d[kvh][t0:t0 + 512, :])
                        else:
                            for q in range(4):
                                OP("tensor", lambda e, q=q, oi=oi: e.transpose(
                                    PTb[:, q * 128:(q + 1) * 128], evq[oi][:, q * 128:(q + 1) * 128],
                                    ident[:]),
                                   waits=[(c_se, EV_OK)] if q == 0 else [],
                                   incs=[c_pe] if q == 3 else [])
                            tokstore(evq[oi], oi % 2, vblk_d[kvh][t0:t0 + 512, :])
        B_KV_PE = c_pe.n

        for tcq in range(2):
            t0c = tcq * 1024
            load_lnT(lnow_d, t0c, [(c_sc, A_SC)])
            OP("sync", lambda e, t0c=t0c: e.dma_start(out=ropec, in_=dram["tb_cosq"][:, t0c:t0c + 1024]),
               waits=[(c_ve, c_ve.n)], incs=[c_lt])
            OP("sync", lambda e, t0c=t0c: e.dma_start(out=ropes, in_=dram["tb_sinq"][:, t0c:t0c + 1024]),
               incs=[c_lt])
            LTN = c_lt.n
            for grp in range(8):
                ocs = [grp * 4 + i for i in range(4)]
                for nh in range(2):
                    t0 = t0c + nh * 512
                    qt0 = t0 // 128
                    for f in range(32):
                        bsel = f % 2
                        wt = wq4[bsel]
                        OP("sync", lambda e, w=wt, f=f, o0=ocs[0]: e.dma_start(
                            out=w, in_=stg["wqkvT_bf"][f * 128:(f + 1) * 128, o0 * 128:o0 * 128 + 512]),
                           waits=[(c_pe, wq_last[bsel]), (c_lt, LTN if f == 0 else 0)],
                           incs=[c_wl2[bsel]])
                        for oi in range(4):
                            OP("tensor", lambda e, f=f, oi=oi, nh=nh, w=wt: e.matmul(
                                P[oi][:], w[:, oi * 128:(oi + 1) * 128],
                                lnT[f][:, nh * 512:(nh + 1) * 512],
                                start=(f == 0), stop=(f == 31)),
                               waits=[(c_wl2[bsel], c_wl2[bsel].n),
                                      (c_se, c_se.n if f == 0 else 0)] if oi == 0 else [],
                               incs=[c_pe] if oi == 3 else [])
                        wq_last[bsel] = c_pe.n
                    MM_DONE = c_pe.n
                    for oi in range(4):
                        hh = ocs[oi]
                        OP("scalar", lambda e, oi=oi: e.copy(out=evq[oi], in_=P[oi][:]),
                           waits=[(c_pe, MM_DONE), (c_ve, c_ve.n), (c_bq, evq_last[oi])], incs=[c_se])
                        OP("scalar", lambda e, oi=oi, hh=hh, qt0=qt0: e.dma_start(
                            out=q_d[qt0:qt0 + 4, :, hh * 128:(hh + 1) * 128].rearrange("t p d -> p t d"),
                            in_=evq[oi].rearrange("p (t d) -> p t d", d=128)),
                           waits=[(c_se, c_se.n)], incs=[c_bq])
                        OP("tensor", lambda e, oi=oi: e.matmul(P[4][:], swap_s[:], evq[oi],
                                                               start=True, stop=True),
                           waits=[(c_se, c_se.n), (c_ve, c_ve.n)], incs=[c_pe])
                        OP("vector", lambda e, nh=nh: e.tensor_tensor(
                            out=ropf[0], in0=P[4][:], in1=ropes[:, nh * 512:(nh + 1) * 512], op=ALU.mult),
                           waits=[(c_pe, c_pe.n)], incs=[c_ve])
                        OP("vector", lambda e, oi=oi, nh=nh: e.tensor_tensor(
                            out=ropf[1], in0=evq[oi], in1=ropec[:, nh * 512:(nh + 1) * 512], op=ALU.mult),
                           incs=[c_ve])
                        evq_last[oi] = c_bq.n
                        OP("vector", lambda e: e.tensor_tensor(out=rkbf, in0=ropf[0], in1=ropf[1],
                                                               op=ALU.add),
                           waits=[(c_ve, c_ve.n), (c_brv, rkbf_last[0])], incs=[c_ve])
                        OP("scalar", lambda e, hh=hh, qt0=qt0: e.dma_start(
                            out=rq_d[qt0:qt0 + 4, :, hh * 128:(hh + 1) * 128].rearrange("t p d -> p t d"),
                            in_=rkbf.rearrange("p (t d) -> p t d", d=128)),
                           waits=[(c_ve, c_ve.n)], incs=[c_brv])
                        rkbf_last[0] = c_brv.n
            for qs in range(8):
                for f in range(32):
                    OP("tensor", lambda e, f=f, qs=qs: e.matmul(
                        P[5][:, 0:96], lnT[f][:, qs * 128:(qs + 1) * 128], wg_s[:, f * 96:(f + 1) * 96],
                        start=(f == 0), stop=(f == 31)),
                       waits=[(c_ld, WG_LD), (c_lt, LTN), (c_ve, c_ve.n)] if f == 0 else [],
                       incs=[c_pe] if f == 31 else [])
                OP("vector", lambda e: e.tensor_tensor(out=gtmp, in0=P[5][:, 0:96], in1=bgbc_s[:],
                                                       op=ALU.add),
                   waits=[(c_pe, c_pe.n), (c_se, c_se.n)], incs=[c_ve])
                OP("scalar", lambda e: e.activation(out=gat[:], in_=gtmp, func=AF.Sigmoid),
                   waits=[(c_ve, c_ve.n), (c_bq, gat_last[0])], incs=[c_se])
                OP("scalar", lambda e, tcq=tcq, qs=qs: e.dma_start(
                    out=gates_d[tcq * 1024 + qs * 128:tcq * 1024 + (qs + 1) * 128, :], in_=gat[:]),
                   waits=[(c_se, c_se.n)], incs=[c_bq])
                gat_last[0] = c_bq.n
        B_PE = c_pe.n
        B_BK = c_bk.n
        B_BQ = c_bq.n
        B_BRV = c_brv.n

    else:
        B_PE = B_BK = B_BQ = B_BRV = 0
    if "C" in phases:
        ktp = cbv(0, 8192)
        vtp = cbv(8192, 8192)
        h1k = cbv(16384, 8192)
        h1v = cbv(24576, 8192)
        w1c = cbv(32768, 2048)
        w1cv = cbv(34816, 2048)
        w2k = cbv(36864, 2048)
        w2v = cbv(38912, 2048)
        cev = cfv(3200, 128)

        if "dbginit" in dbg:
            OP("scalar", lambda e: e.dma_start(out=dram["dbgcv0i"][:], in_=cv0[:]),
               waits=[(c_pe, B_PE), (c_se, c_se.n), (c_ve, c_ve.n)], incs=[c_st])
        OP("sync", lambda e: e.dma_start(out=w2k.rearrange("p (j c) -> p j c", j=16),
                                         in_=stg["k_w2T_bf"].rearrange("(j p) c -> p j c", p=128)),
           waits=[(c_wk, wdone["k_w2T"]), (c_pe, B_PE)], incs=[c_w2])
        OP("sync", lambda e: e.dma_start(out=w2v.rearrange("p (j c) -> p j c", j=16),
                                         in_=stg["v_w2T_bf"].rearrange("(j p) c -> p j c", p=128)),
           waits=[(c_wk, wdone["v_w2T"])], incs=[c_w2])
        W2_OK = c_w2.n

        for pr in range(4):
            kvhs = (2 * pr, 2 * pr + 1)
            for j in range(16):
                for ii, kvh in enumerate(kvhs):
                    OP("sync", lambda e, j=j, ii=ii, kvh=kvh: e.dma_start(
                        out=ktp[:, j * 512 + ii * 256:j * 512 + (ii + 1) * 256],
                        in_=kblk_d[kvh].rearrange("(b j) d -> j b d", j=16)[j], transpose=True),
                       waits=[(c_pe, c_pe.n), (c_bk, B_BK), (c_ve, c_ve.n)]
                       if (j == 0 and ii == 0) else [],
                       incs=[c_lt])
                    OP("sync", lambda e, j=j, ii=ii, kvh=kvh: e.dma_start(
                        out=vtp[:, j * 512 + ii * 256:j * 512 + (ii + 1) * 256],
                        in_=vblk_d[kvh].rearrange("(b j) d -> j b d", j=16)[j], transpose=True),
                       incs=[c_lt])
            LTC = c_lt.n
            for j in range(16):
                for ii, kvh in enumerate(kvhs):
                    OP("vector", lambda e, j=j, ii=ii, kvh=kvh: e.tensor_scalar(
                        out=ktp[:, j * 512 + ii * 256:j * 512 + (ii + 1) * 256],
                        in0=ktp[:, j * 512 + ii * 256:j * 512 + (ii + 1) * 256],
                        scalar1=kposT_s[:, kvh * 16 + j:kvh * 16 + j + 1], scalar2=None,
                        op0=ALU.add, op1=ALU.bypass),
                       waits=[(c_lt, LTC)] if (j == 0 and ii == 0) else [], incs=[c_ve])
                    OP("vector", lambda e, j=j, ii=ii, kvh=kvh: e.tensor_scalar(
                        out=vtp[:, j * 512 + ii * 256:j * 512 + (ii + 1) * 256],
                        in0=vtp[:, j * 512 + ii * 256:j * 512 + (ii + 1) * 256],
                        scalar1=vposT_s[:, kvh * 16 + j:kvh * 16 + j + 1], scalar2=None,
                        op0=ALU.add, op1=ALU.bypass), incs=[c_ve])
            RHS_OK = c_ve.n
            if dbgktp_d is not None and pr == 0:
                OP("scalar", lambda e: e.dma_start(out=dbgktp_d[0], in_=ktp), waits=[(c_ve, RHS_OK)], incs=[c_st])
            for hoc in range(16):
                OP("sync", lambda e, hoc=hoc: e.dma_start(
                    out=w1c.rearrange("p (j c) -> p j c", j=16),
                    in_=stg["k_w1T_bf"][:, hoc * 128:(hoc + 1) * 128].rearrange("(j p) c -> p j c", p=128)),
                   waits=[(c_wk, wdone["k_w1T"] if (pr == 0 and hoc == 0) else 0),
                          (c_pe, c_pe.n)], incs=[c_wl2[0]])
                OP("sync", lambda e, hoc=hoc: e.dma_start(
                    out=w1cv.rearrange("p (j c) -> p j c", j=16),
                    in_=stg["v_w1T_bf"][:, hoc * 128:(hoc + 1) * 128].rearrange("(j p) c -> p j c", p=128)),
                   incs=[c_wl2[1]])
                WOK = (c_wl2[0].n, c_wl2[1].n)
                for j in range(16):
                    OP("tensor", lambda e, j=j: e.matmul(
                        P[0][:], w1c[:, j * 128:(j + 1) * 128], ktp[:, j * 512:(j + 1) * 512],
                        start=(j == 0), stop=(j == 15)),
                       waits=[(c_wl2[0], WOK[0]), (c_ve, RHS_OK), (c_se, c_se.n)] if j == 0 else [],
                       incs=[c_pe] if j == 15 else [])
                OP("scalar", lambda e, hoc=hoc: e.activation(
                    out=h1k[:, hoc * 512:(hoc + 1) * 512], in_=P[0][:], func=AF.Relu,
                    bias=kb1_s[:, hoc:hoc + 1]),
                   waits=[(c_pe, c_pe.n)], incs=[c_se])
                for j in range(16):
                    OP("tensor", lambda e, j=j: e.matmul(
                        P[1][:], w1cv[:, j * 128:(j + 1) * 128], vtp[:, j * 512:(j + 1) * 512],
                        start=(j == 0), stop=(j == 15)),
                       waits=[(c_wl2[1], WOK[1])] if j == 0 else [], incs=[c_pe] if j == 15 else [])
                OP("scalar", lambda e, hoc=hoc: e.activation(
                    out=h1v[:, hoc * 512:(hoc + 1) * 512], in_=P[1][:], func=AF.Relu,
                    bias=vb1_s[:, hoc:hoc + 1]),
                   waits=[(c_pe, c_pe.n)], incs=[c_se])
            H1_OK = c_se.n
            if dbgktp_d is not None and pr == 0:
                OP("scalar", lambda e: e.dma_start(out=dbgktp_d[1], in_=h1k), waits=[(c_se, H1_OK)], incs=[c_st])
            for ii, kvh in enumerate(kvhs):
                for hc in range(16):
                    OP("tensor", lambda e, hc=hc, ii=ii: e.matmul(
                        P[2][:, 0:256], w2k[:, hc * 128:(hc + 1) * 128],
                        h1k[:, hc * 512 + ii * 256:hc * 512 + (ii + 1) * 256],
                        start=(hc == 0), stop=(hc == 15)),
                       waits=[(c_se, H1_OK), (c_w2, W2_OK),
                              (c_ve, c_ve.n)] if hc == 0 else [],
                       incs=[c_pe] if hc == 15 else [])
                OP("vector", lambda e, kvh=kvh: e.tensor_scalar(
                    out=ckT[:, kvh * 264:kvh * 264 + 256], in0=P[2][:, 0:256],
                    scalar1=kb2_s[:], scalar2=None, op0=ALU.add, op1=ALU.bypass),
                   waits=[(c_pe, c_pe.n)], incs=[c_ve])
                OP("vector", lambda e, kvh=kvh: e.tensor_copy(
                    out=ckT[:, kvh * 264 + 256:kvh * 264 + 257], in_=memk_s[:, kvh:kvh + 1]),
                   incs=[c_ve])
                OP("vector", lambda e, kvh=kvh: e.memset(ckT[:, kvh * 264 + 257:kvh * 264 + 264], 0.0),
                   incs=[c_ve])
                for half, cvt in ((0, cv0), (1, cv1)):
                    for hc in range(16):
                        OP("tensor", lambda e, hc=hc, ii=ii, half=half: e.matmul(
                            P[3][:, 0:128],
                            h1v[:, hc * 512 + ii * 256 + half * 128:hc * 512 + ii * 256 + half * 128 + 128],
                            w2v[:, hc * 128:(hc + 1) * 128],
                            start=(hc == 0), stop=(hc == 15)),
                           waits=[(c_se, H1_OK), (c_w2, W2_OK),
                                  (c_ve, c_ve.n)] if hc == 0 else [],
                           incs=[c_pe] if hc == 15 else [])
                    OP("vector", lambda e: e.tensor_tensor(out=cev, in0=P[3][:, 0:128],
                                                           in1=vb2bc_s[:], op=ALU.add),
                       waits=[(c_pe, c_pe.n), (c_ve, c_ve.n)], incs=[c_ve])
                    OP("vector", lambda e, kvh=kvh, cvt=cvt: e.tensor_copy(
                        out=cvt[:, kvh * 129:kvh * 129 + 128], in_=cev),
                       waits=[(c_ve, c_ve.n)], incs=[c_ve])
                OP("gpsimd", lambda e, kvh=kvh: e.dma_start(
                    out=cv2[0:1, kvh * 129:kvh * 129 + 128], in_=dram["memv"][kvh]),
                   waits=[(c_ve, V_INIT)], incs=[c_gw])
        if dbgck_d is not None:
            OP("scalar", lambda e: e.dma_start(out=dbgck_d[:], in_=ckT[:]),
               waits=[(c_ve, c_ve.n), (c_gw, c_gw.n)], incs=[c_st])
            OP("scalar", lambda e: e.dma_start(out=dbgcv_d[0], in_=cv0[:]), incs=[c_st])
            OP("scalar", lambda e: e.dma_start(out=dbgcv_d[1], in_=cv1[:]), incs=[c_st])
            OP("scalar", lambda e: e.dma_start(out=dbgcv_d[2][0:8, :], in_=cv2[:]), incs=[c_st])
        C_PE, C_VE, C_SE, C_GD = c_pe.n, c_ve.n, c_se.n, c_gw.n

    else:
        C_PE, C_VE, C_SE, C_GD = c_pe.n, c_ve.n, c_se.n, c_gw.n
    if "D" in phases:
        lnT2 = [cbv(i * 512, 512) for i in range(32)]
        rqT = cbv(0, 4096)
        qT = cbv(4096, 4096)
        est_f = cbv(8192, 2048)
        kgb = [cbv(10240, 2048), cbv(12288, 2048)]
        vgb = [cbv(14336, 2048), cbv(16384, 2048)]
        ktg = cbv(18432, 512)
        est_c = cbv(18944, 256)
        cm1t = cbv(19200, 256)
        smat = cbv(19456, 128)
        rkTw = cbv(19584, 192)
        est_sA = cbv(19776, 128)
        hgt4 = [cbv(19904 + i * 256, 256) for i in range(4)]
        ln2tl = cbv(20928, 4096)
        wtl = [cbv(25024, 512), cbv(25536, 512)]
        lgt = [cbv(26048, 256), cbv(26304, 256)]
        bft = [cbv(26560, 512), cbv(27072, 512)]
        combT = cbv(28672, 16384)
        xres = [cfv(0, 512), cfv(512, 512)]
        mtmp = [cfv(1024, 512), cfv(6960, 512)]
        xmid = cfv(1536, 4096)
        msum = cfv(5632, 256)
        ecf = cfv(5888, 256)
        ecq = cfv(6144, 256)
        outt = [cfv(6400, 512), cfv(7472, 512)]
        selscr = cfv(6912, 64)
        mt_last = [0, 0]
        ot_last = [0, 0]
        xres_last = [0, 0]
        l2_last = [0]
        wtl_last = [0, 0]
        lgt_last = [0, 0]
        kv_last = [0, 0]
        idx_gd = [(0, 0)]
        seltap = [0]

        for qt in range(d_qtiles):
            grp, qip = qt // 4, qt % 4
            fence = [(c_ve, C_VE), (c_se, C_SE), (c_gw, C_GD), (c_bk, B_BK),
                     (c_brv, B_BRV)] if qt == 0 else []
            OP("sync", lambda e, qt=qt: e.dma_start(out=rqT, in_=rq_d[qt]),
               waits=fence + [(c_pe, c_pe.n), (c_bq, B_BQ if qt == 0 else 0)], incs=[c_lt])
            OP("sync", lambda e, qt=qt: e.dma_start(out=qT, in_=q_d[qt]), incs=[c_lt])
            OP("sync", lambda e, qt=qt: e.dma_start(out=cm1t, in_=dram["tb_cm1"][qt]), incs=[c_lt])
            OP("sync", lambda e, qt=qt: e.dma_start(out=smat, in_=dram["tb_sma"][qt]), incs=[c_lt])
            OP("sync", lambda e, qt=qt: e.dma_start(out=gat[:], in_=gates_d[qt * 128:(qt + 1) * 128, :]),
               waits=[(c_ve, c_ve.n)], incs=[c_lt])
            D0_OK = c_lt.n
            OP("vector", lambda e: e.tensor_copy(out=gatb[:], in_=gat[:]),
               waits=[(c_lt, D0_OK), (c_ve, c_ve.n)], incs=[c_ve])
            GATB = c_ve.n

            kvh_n = KVH if (dlim is None or qt < d_qtiles - 1) else dlim[0]
            for kvh in range(kvh_n):
                hd0 = 4 * kvh
                sel_lvl = 99 if (dlim is None or qt < d_qtiles - 1) else dlim[1]
                for g in range(4):
                    head = hd0 + g
                    OP("tensor", lambda e, kvh=kvh, head=head: e.matmul(
                        P[4][:, 0:128], ckT[:, kvh * 264:kvh * 264 + 128],
                        qT[:, head * 128:(head + 1) * 128], start=True, stop=True),
                       waits=[(c_lt, D0_OK), (c_ve, c_ve.n), (c_se, c_se.n)], incs=[])
                    OP("tensor", lambda e, kvh=kvh, head=head: e.matmul(
                        P[4][:, 128:256], ckT[:, kvh * 264 + 128:kvh * 264 + 256],
                        qT[:, head * 128:(head + 1) * 128], start=True, stop=True), incs=[])
                    OP("tensor", lambda e, kvh=kvh, head=head: e.matmul(
                        P[4][0:8, 256:384], ckT[:, kvh * 264 + 256:kvh * 264 + 264],
                        qT[:, head * 128:(head + 1) * 128], start=True, stop=True), incs=[c_pe])
                    CS_OK = c_pe.n
                    OP("scalar", lambda e: e.activation(out=est_c, in_=P[4][:, 0:256],
                                                        func=AF.Exp, scale=SC),
                       waits=[(c_pe, CS_OK), (c_ve, c_ve.n)], incs=[c_se])
                    OP("scalar", lambda e: e.activation(out=est_c2[:], in_=P[4][0:8, 256:384],
                                                        func=AF.Exp, scale=SC), incs=[c_se])
                    if g == 0:
                        OP("vector", lambda e: e.tensor_copy(out=msum, in_=P[4][:, 0:256]),
                           waits=[(c_se, c_se.n)], incs=[c_ve])
                    else:
                        OP("vector", lambda e: e.tensor_tensor(out=msum, in0=P[4][:, 0:256],
                                                               in1=msum, op=ALU.add),
                           waits=[(c_se, c_se.n)], incs=[c_ve])
                    OP("vector", lambda e: e.tensor_tensor(out=est_c, in0=est_c, in1=cm1t, op=ALU.mult),
                       waits=[(c_se, c_se.n)], incs=[c_ve])
                    OP("vector", lambda e: e.tensor_tensor(out=est_c2[:], in0=est_c2[:],
                                                           in1=cm2_s[:], op=ALU.mult), incs=[c_ve])
                    EMSK = c_ve.n
                    OP("tensor", lambda e, kvh=kvh: e.matmul(
                        P[5][:, 0:129], est_c[:, 0:128], cv0[:, kvh * 129:(kvh + 1) * 129],
                        start=True, stop=False),
                       waits=[(c_ve, EMSK)], incs=[])
                    OP("tensor", lambda e, kvh=kvh: e.matmul(
                        P[5][:, 0:129], est_c[:, 128:256], cv1[:, kvh * 129:(kvh + 1) * 129],
                        start=False, stop=False), incs=[])
                    OP("tensor", lambda e, kvh=kvh: e.matmul(
                        P[5][:, 0:129], est_c2[:], cv2[:, kvh * 129:(kvh + 1) * 129],
                        start=False, stop=True), incs=[c_pe])
                    OP("vector", lambda e: e.reciprocal(out=rz[:], in_=P[5][:, 128:129]),
                       waits=[(c_pe, c_pe.n), (c_ve, GATB)], incs=[c_ve])
                    OP("vector", lambda e, head=head: e.tensor_tensor(
                        out=wcol[:], in0=gat[:, 3 * head:3 * head + 1], in1=rz[:], op=ALU.mult),
                       waits=[(c_ve, c_ve.n)], incs=[c_ve])
                    OP("vector", lambda e, head=head: e.tensor_scalar(
                        out=acc[:, head * 128:(head + 1) * 128], in0=P[5][:, 0:128],
                        scalar1=wcol[:], scalar2=None, op0=ALU.mult, op1=ALU.bypass),
                       waits=[(c_ve, c_ve.n)], incs=[c_ve])
                if sel_lvl < 1:
                    continue
                OP("scalar", lambda e: e.activation(out=ecf, in_=msum, func=AF.Exp, scale=SC / 4.0),
                   waits=[(c_ve, c_ve.n)], incs=[c_se])
                OP("vector", lambda e: e.tensor_tensor(out=ecf, in0=ecf, in1=cm1t, op=ALU.mult),
                   waits=[(c_se, c_se.n)], incs=[c_ve])
                for cc in range(2):
                    OP("tensor", lambda e, cc=cc: e.transpose(
                        P[6][:, cc * 128:(cc + 1) * 128], ecf[:, cc * 128:(cc + 1) * 128], identF[:]),
                       waits=[(c_ve, c_ve.n), (c_se, c_se.n)] if cc == 0 else [],
                       incs=[c_pe] if cc == 1 else [])
                OP("scalar", lambda e: e.copy(out=ecq, in_=P[6][:, 0:256]),
                   waits=[(c_pe, c_pe.n)], incs=[c_se])
                if sel_lvl < 2:
                    continue
                OP("vector", lambda e: e.reduce_sum(out=zimp[:], in_=ecq, axis=AX.X),
                   waits=[(c_se, c_se.n)], incs=[c_ve])
                OP("vector", lambda e: e.tensor_scalar(out=timp[:], in0=zimp[:], scalar1=1e-10,
                                                       scalar2=None, op0=ALU.mult, op1=ALU.bypass),
                   waits=[(c_ve, c_ve.n)], incs=[c_ve])
                if sel_lvl < 3:
                    continue
                OP("vector", lambda e: e.max(out=top8[:], in_=ecq), waits=[(c_se, c_se.n)], incs=[c_ve])
                OP("vector", lambda e: e.max_index(out=mi8[:], in_max=top8[:], in_values=ecq),
                   waits=[(c_ve, c_ve.n)], incs=[c_ve])
                if sel_lvl < 4:
                    continue
                OP("vector", lambda e: e.tensor_copy(out=selfsel[:, 0:4], in_=mi8[:, 0:4]),
                   waits=[(c_ve, c_ve.n)], incs=[c_ve])
                OP("vector", lambda e: e.tensor_scalar(out=selfsel[:, 4:8], in0=top8[:, 0:4],
                                                       scalar1=timp[:], scalar2=None,
                                                       op0=ALU.is_gt, op1=ALU.bypass),
                   waits=[(c_ve, c_ve.n)], incs=[c_ve])
                SEL_OK = c_ve.n
                if sel_lvl < 5:
                    continue
                for gi in range(4):
                    OP("tensor", lambda e, gi=gi: e.matmul(
                        P[6][:, 256 + gi * 8:256 + (gi + 1) * 8],
                        perm_s[:, gi * 128:(gi + 1) * 128], selfsel[:], start=True, stop=True),
                       waits=[(c_ve, SEL_OK)] if gi == 0 else [], incs=[c_pe] if gi == 3 else [])
                OP("vector", lambda e: e.tensor_tensor(out=selscr[:, 0:32], in0=P[6][:, 256:288],
                                                       in1=m48_s[:], op=ALU.mult),
                   waits=[(c_pe, c_pe.n)], incs=[c_ve])
                sv4 = selscr[:, 0:32].rearrange("p (a b) -> p a b", b=4)
                OP("vector", lambda e, sv4=sv4: e.tensor_tensor(
                    out=selred[:], in0=sv4[:, :, 0], in1=sv4[:, :, 1], op=ALU.add),
                   waits=[(c_ve, c_ve.n), (c_st, seltap[0])], incs=[c_ve])
                OP("vector", lambda e, sv4=sv4: e.tensor_tensor(
                    out=selred[:], in0=selred[:], in1=sv4[:, :, 2], op=ALU.add),
                   waits=[(c_ve, c_ve.n)], incs=[c_ve])
                OP("vector", lambda e, sv4=sv4: e.tensor_tensor(
                    out=selred[:], in0=selred[:], in1=sv4[:, :, 3], op=ALU.add),
                   waits=[(c_ve, c_ve.n)], incs=[c_ve])
                if sel_lvl < 6:
                    continue
                OP("vector", lambda e: e.tensor_scalar(
                    out=idxf[:], in0=selred[:].rearrange("p (a b) -> p a b", b=2)[:, :, 0],
                    scalar1=255.0, scalar2=0.0, op0=ALU.min, op1=ALU.max),
                   waits=[(c_ve, c_ve.n)], incs=[c_ve])
                OP("vector", lambda e: e.tensor_copy(out=idxcols[:], in_=idxf[:]),
                   waits=[(c_ve, c_ve.n), (c_gf2[0], idx_gd[0][0]),
                          (c_gf2[1], idx_gd[0][1])], incs=[c_ve])
                OP("vector", lambda e: e.tensor_copy(
                    out=fselcols[:], in_=selred[:].rearrange("p (a b) -> p a b", b=2)[:, :, 1]),
                   incs=[c_ve])
                if sel_lvl < 7:
                    continue
                OP("vector", lambda e, kvh=kvh: e.tensor_copy(
                    out=gsel[:, 0:4],
                    in_=gatb[:].rearrange("p (h k) -> p h k", k=3)[:, 4 * kvh:4 * kvh + 4, 1]),
                   waits=[(c_ve, c_ve.n)], incs=[c_ve])
                GSEL = c_ve.n
                for gi in range(4):
                    OP("tensor", lambda e, gi=gi: e.matmul(
                        P[6][:, 288 + gi * 8:288 + (gi + 1) * 8],
                        permg_s[:, gi * 128:(gi + 1) * 128], gsel[:],
                        start=True, stop=True),
                       waits=[(c_ve, GSEL)] if gi == 0 else [], incs=[c_pe] if gi == 3 else [])
                OP("vector", lambda e: e.tensor_tensor(out=selscr[:, 32:64], in0=P[6][:, 288:320],
                                                       in1=mg4b_s[:], op=ALU.mult),
                   waits=[(c_pe, c_pe.n)], incs=[c_ve])
                gv8 = selscr[:, 32:64].rearrange("p (a b) -> p a b", b=8)
                OP("vector", lambda e, gv8=gv8: e.tensor_tensor(
                    out=gpred[:], in0=gv8[:, :, 0], in1=gv8[:, :, 1], op=ALU.add),
                   waits=[(c_ve, c_ve.n)], incs=[c_ve])
                OP("vector", lambda e, gv8=gv8: e.tensor_tensor(
                    out=gpred[:], in0=gpred[:], in1=gv8[:, :, 2], op=ALU.add),
                   waits=[(c_ve, c_ve.n)], incs=[c_ve])
                OP("vector", lambda e, gv8=gv8: e.tensor_tensor(
                    out=gpred[:], in0=gpred[:], in1=gv8[:, :, 3], op=ALU.add),
                   waits=[(c_ve, c_ve.n)], incs=[c_ve])
                IDX_OK = c_ve.n
                if dbgsel_d is not None:
                    OP("scalar", lambda e, qt=qt, kvh=kvh: e.dma_start(
                        out=dbgsel_d[qt, :, kvh * 8:(kvh + 1) * 8], in_=selred[:]),
                       waits=[(c_ve, IDX_OK)], incs=[c_st])
                    seltap[0] = c_st.n
                    OP("scalar", lambda e: e.nop(), waits=[(c_st, c_st.n)], incs=[c_ve])

                if "s" in comps:
                    gw_wait = [(c_pe, c_pe.n), (c_bk, B_BK if (qt == 0 and kvh == 0) else 0)]
                    rkfl = rkblk_d.rearrange("h s d -> (h s) d")
                    vbfl = vblk_d.rearrange("h s d -> (h s) d")
                    OP("gpsimd", lambda e, kvh=kvh, qt=qt, rkfl=rkfl: e.indirect_dma_start(
                        out=rkwA[:], out_offset=None, in_=rkfl,
                        in_offset=bass.IndirectOffsetOnAxis(ap=swa_s[:, qt:qt + 1], axis=0),
                        element_offset=kvh * S * DH),
                       waits=gw_wait, incs=[c_gw])
                    OP("gpsimd", lambda e, kvh=kvh, qt=qt, rkfl=rkfl: e.indirect_dma_start(
                        out=rkwB[:], out_offset=None, in_=rkfl,
                        in_offset=bass.IndirectOffsetOnAxis(ap=swb_s[:, qt:qt + 1], axis=0),
                        element_offset=kvh * S * DH), incs=[c_gw])
                    OP("gpsimd", lambda e, kvh=kvh, qt=qt, vbfl=vbfl: e.indirect_dma_start(
                        out=vwA[:, 0:128], out_offset=None, in_=vbfl,
                        in_offset=bass.IndirectOffsetOnAxis(ap=swa_s[:, qt:qt + 1], axis=0),
                        element_offset=kvh * S * DH), incs=[c_gw])
                    OP("gpsimd", lambda e, kvh=kvh, qt=qt, vbfl=vbfl: e.indirect_dma_start(
                        out=vwB[:, 0:128], out_offset=None, in_=vbfl,
                        in_offset=bass.IndirectOffsetOnAxis(ap=swb_s[:, qt:qt + 1], axis=0),
                        element_offset=kvh * S * DH), incs=[c_gw])
                    GW_OK = c_gw.n
                    OP("tensor", lambda e: e.transpose(PTb[:, 0:128], rkwA[:], ident[:]),
                       waits=[(c_gw, GW_OK), (c_se, c_se.n)], incs=[])
                    OP("tensor", lambda e: e.transpose(PTb[:, 128:256], rkwB[:], ident[:]), incs=[c_pe])
                    OP("scalar", lambda e: e.copy(out=rkTw[:, 0:128], in_=PTb[:, 0:128]),
                       waits=[(c_pe, c_pe.n)], incs=[c_se])
                    OP("scalar", lambda e: e.copy(out=rkTw[:, 128:192], in_=PTb[:, 128:192]), incs=[c_se])
                    RKT_OK = c_se.n
                    if dbgwin_d is not None and qt == 0 and kvh == 0:
                        OP("scalar", lambda e: e.dma_start(out=dbgwin_d[0, :, 0:128], in_=rkwA[:]),
                           waits=[(c_se, RKT_OK)], incs=[c_st])
                        OP("scalar", lambda e: e.dma_start(out=dbgwin_d[1, :, 0:128], in_=rkwB[:]), incs=[c_st])
                        OP("scalar", lambda e: e.dma_start(out=dbgwin_d[2, :, 0:129], in_=vwA[:]), incs=[c_st])
                        OP("scalar", lambda e: e.dma_start(out=dbgwin_d[3], in_=rkTw[:]), incs=[c_st])
                        OP("scalar", lambda e: e.nop(), waits=[(c_st, c_st.n)], incs=[c_ve])
                        seltap[0] = c_st.n
                    for g in range(4):
                        head = hd0 + g
                        OP("tensor", lambda e, head=head: e.matmul(
                            P[4][:, 0:128], rkTw[:, 0:128], rqT[:, head * 128:(head + 1) * 128],
                            start=True, stop=True),
                           waits=[(c_se, c_se.n), (c_ve, c_ve.n)], incs=[])
                        OP("tensor", lambda e, head=head: e.matmul(
                            P[4][0:64, 128:256], rkTw[:, 128:192], rqT[:, head * 128:(head + 1) * 128],
                            start=True, stop=True), incs=[c_pe])
                        OP("scalar", lambda e: e.activation(out=est_sA, in_=P[4][:, 0:128],
                                                            func=AF.Exp, scale=SC),
                           waits=[(c_pe, c_pe.n)], incs=[c_se])
                        OP("scalar", lambda e: e.activation(out=est_sB[:], in_=P[4][0:64, 128:256],
                                                            func=AF.Exp, scale=SC), incs=[c_se])
                        OP("vector", lambda e: e.tensor_tensor(out=est_sA, in0=est_sA, in1=smat, op=ALU.mult),
                           waits=[(c_se, c_se.n)], incs=[c_ve])
                        OP("vector", lambda e: e.tensor_tensor(out=est_sB[:], in0=est_sB[:], in1=smb_s[:],
                                                               op=ALU.mult), incs=[c_ve])
                        SM_OK = c_ve.n
                        OP("tensor", lambda e: e.matmul(P[5][:, 0:129], est_sA, vwA[:], start=True, stop=False),
                           waits=[(c_ve, SM_OK)], incs=[])
                        OP("tensor", lambda e: e.matmul(P[5][:, 0:129], est_sB[:], vwB[0:64, :],
                                                        start=False, stop=True), incs=[c_pe])
                        OP("vector", lambda e: e.reciprocal(out=rz[:], in_=P[5][:, 128:129]),
                           waits=[(c_pe, c_pe.n)], incs=[c_ve])
                        OP("vector", lambda e, head=head: e.tensor_tensor(
                            out=wcol[:], in0=gat[:, 3 * head + 2:3 * head + 3], in1=rz[:], op=ALU.mult),
                           waits=[(c_ve, c_ve.n)], incs=[c_ve])
                        OP("vector", lambda e, head=head: e.scalar_tensor_tensor(
                            out=acc[:, head * 128:(head + 1) * 128], in0=P[5][:, 0:128], scalar=wcol[:],
                            in1=acc[:, head * 128:(head + 1) * 128], op0=ALU.mult, op1=ALU.add),
                           waits=[(c_ve, c_ve.n), (c_st, seltap[0])], incs=[c_ve])

                if "f" in comps:
                    for gi in range(4):
                        bsel = gi % 2
                        kb, vb = kgb[bsel], vgb[bsel]
                        rkbl = rkblk_d.rearrange("h (b t) d -> (h b) (t d)", t=16)
                        vbbl = vblk_d.rearrange("h (b t) d -> (h b) (t d)", t=16)
                        OP("gpsimd", lambda e, kvh=kvh, gi=gi, kb=kb, rkbl=rkbl: e.indirect_dma_start(
                            out=kb, out_offset=None, in_=rkbl,
                            in_offset=bass.IndirectOffsetOnAxis(ap=idxcols[:, gi:gi + 1], axis=0),
                            element_offset=kvh * NBLK * 2048),
                           waits=[(c_ve, IDX_OK), (c_pe, kv_last[bsel])], incs=[c_gf2[bsel]])
                        OP("gpsimd", lambda e, kvh=kvh, gi=gi, vb=vb, vbbl=vbbl: e.indirect_dma_start(
                            out=vb, out_offset=None, in_=vbbl,
                            in_offset=bass.IndirectOffsetOnAxis(ap=idxcols[:, gi:gi + 1], axis=0),
                            element_offset=kvh * NBLK * 2048),
                           incs=[c_gf2[bsel]])
                        GF_OK = c_gf2[bsel].n
                        rq4 = rqT.rearrange("p (H q) -> p H q", H=HQ)[:, hd0:hd0 + 4,
                                                                     gi * 32:(gi + 1) * 32]
                        for jj in range(4):
                            for q in range(4):
                                OP("tensor", lambda e, jj=jj, q=q, kb=kb: e.transpose(
                                    PTb[:, q * 128:(q + 1) * 128],
                                    kb[:, (jj * 4 + q) * 128:(jj * 4 + q + 1) * 128], ident[:]),
                                   waits=[(c_gf2[bsel], GF_OK), (c_se, c_se.n)] if q == 0 else [],
                                   incs=[c_pe] if q == 3 else [])
                            OP("scalar", lambda e: e.copy(out=ktg, in_=PTb[:]),
                               waits=[(c_pe, c_pe.n)], incs=[c_se])
                            KT_OK = c_se.n
                            for q in range(4):
                                OP("tensor", lambda e, q=q, rq4=rq4: e.matmul(
                                    P[4][:, q * 128:(q + 1) * 128], ktg[:, q * 128:(q + 1) * 128],
                                    rq4, start=True, stop=True),
                                   waits=[(c_se, KT_OK)] if q == 0 else [],
                                   incs=[c_pe] if q == 3 else [])
                            OP("scalar", lambda e, jj=jj: e.activation(
                                out=est_f[:, jj * 512:(jj + 1) * 512], in_=P[4][:], func=AF.Exp, scale=SC),
                               waits=[(c_pe, c_pe.n), (c_ve, c_ve.n)], incs=[c_se])
                        OP("tensor", lambda e, gi=gi, rq4=rq4: e.matmul(
                            P[6][0:32, 384:512], rkTw[:, 64 + gi * 32:64 + (gi + 1) * 32], rq4,
                            start=True, stop=True),
                           waits=[(c_se, c_se.n)], incs=[c_pe])
                        OP("scalar", lambda e: e.activation(out=est_fo[:], in_=P[6][0:32, 384:512],
                                                            func=AF.Exp, scale=SC),
                           waits=[(c_pe, c_pe.n)], incs=[c_se])
                        OP("vector", lambda e: e.tensor_tensor(out=est_fo[:], in0=est_fo[:],
                                                               in1=mdiag_s[:], op=ALU.mult),
                           waits=[(c_se, c_se.n)], incs=[c_ve])
                        OP("vector", lambda e, gi=gi: e.scalar_tensor_tensor(
                            out=est_f, in0=est_f, scalar=fselcols[:, gi:gi + 1], in1=mown_s[:],
                            op0=ALU.mult, op1=ALU.mult),
                           waits=[(c_se, c_se.n)], incs=[c_ve])
                        EF_OK = c_ve.n
                        if dbgfin_d is not None and qt == 0 and kvh == 0 and gi == 0:
                            OP("scalar", lambda e: e.dma_start(
                                out=dbgfin_d[:, 0:1], in_=fselcols[:, 0:1],
                                allow_slow_non_contiguous=True),
                               waits=[(c_ve, EF_OK)], incs=[c_st])
                            OP("scalar", lambda e: e.dma_start(out=dbgfin_d[:, 1:2], in_=gpred[:, 0:1],
                                                               allow_slow_non_contiguous=True),
                               incs=[c_st])
                            OP("scalar", lambda e: e.dma_start(out=dbgfin_d[:, 2:3], in_=rz[:],
                                                               allow_slow_non_contiguous=True),
                               incs=[c_st])
                            OP("scalar", lambda e: e.dma_start(out=dbgfin2_d[0], in_=kb), incs=[c_st])
                            OP("scalar", lambda e: e.dma_start(out=dbgfin2_d[1], in_=est_f), incs=[c_st])
                            OP("scalar", lambda e: e.dma_start(out=dbgidx_d[:], in_=idxcols[:]),
                               incs=[c_st])
                            OP("scalar", lambda e: e.nop(), waits=[(c_st, c_st.n)], incs=[c_ve])
                            seltap[0] = c_st.n
                        for j in range(16):
                            OP("tensor", lambda e, j=j, vb=vb: e.matmul(
                                P[5][:, 0:128], est_f[:, j * 128:(j + 1) * 128],
                                vb[:, j * 128:(j + 1) * 128], start=(j == 0), stop=False),
                               waits=[(c_ve, EF_OK)] if j == 0 else [], incs=[])
                        vown = vwA if gi < 2 else vwB
                        vo0 = 64 + gi * 32 if gi < 2 else (gi - 2) * 32
                        OP("vector", lambda e, vown=vown, vo0=vo0: e.tensor_copy(
                            out=vown_s[:], in_=vown[vo0:vo0 + 32, :]),
                           waits=[(c_pe, c_pe.n)], incs=[c_ve])
                        VOWN_OK = c_ve.n
                        OP("tensor", lambda e: e.matmul(
                            P[5][:, 0:128], est_fo[:], vown_s[0:32, 0:128],
                            start=False, stop=True),
                           waits=[(c_ve, VOWN_OK)], incs=[])
                        for j in range(16):
                            OP("tensor", lambda e, j=j: e.matmul(
                                P[0][:, 0:1], est_f[:, j * 128:(j + 1) * 128], ones1[:],
                                start=(j == 0), stop=False),
                               waits=[(c_ve, c_ve.n)] if j == 0 else [], incs=[])
                        OP("tensor", lambda e: e.matmul(P[0][:, 0:1], est_fo[:], ones1[0:32, :],
                                                        start=False, stop=True), incs=[c_pe])
                        PV_OK = c_pe.n
                        kv_last[bsel] = PV_OK
                        if gi == 3:
                            idx_gd[0] = (c_gf2[0].n, c_gf2[1].n)
                        OP("vector", lambda e: e.reciprocal(out=rz[:], in_=P[0][:, 0:1]),
                           waits=[(c_pe, PV_OK)], incs=[c_ve])
                        OP("vector", lambda e, gi=gi: e.tensor_tensor(out=wcol[:], in0=gpred[:, gi:gi + 1],
                                                                      in1=rz[:], op=ALU.mult),
                           waits=[(c_ve, c_ve.n)], incs=[c_ve])
                        OP("vector", lambda e: e.tensor_scalar(
                            out=pt_a[:], in0=P[5][:, 0:128], scalar1=wcol[:], scalar2=None,
                            op0=ALU.mult, op1=ALU.bypass),
                           waits=[(c_ve, c_ve.n)], incs=[c_ve])
                        for g in range(4):
                            OP("vector", lambda e, g=g, gi=gi: e.tensor_copy(
                                out=pt_b[gi * 32:(gi + 1) * 32, g * 128:(g + 1) * 128],
                                in_=pt_a[g * 32:(g + 1) * 32, :]),
                               waits=[(c_ve, c_ve.n)] if g == 0 else [],
                               incs=[c_ve] if g == 3 else [])
                        OP("vector", lambda e, gi=gi, hd0=hd0: e.tensor_tensor(
                            out=acc[gi * 32:(gi + 1) * 32, hd0 * 128:hd0 * 128 + 512],
                            in0=pt_b[gi * 32:(gi + 1) * 32, :],
                            in1=acc[gi * 32:(gi + 1) * 32, hd0 * 128:hd0 * 128 + 512],
                            op=ALU.add),
                           waits=[(c_ve, c_ve.n), (c_st, seltap[0])], incs=[c_ve])

                if dbgacc_d is not None and qt == 0:
                    OP("scalar", lambda e: e.dma_start(out=dbgacc_d[2], in_=acc[:]),
                       waits=[(c_ve, c_ve.n)], incs=[c_st])
                    seltap[0] = c_st.n
            for hq in range(HQ // 4):
                for k4 in range(4):
                    head = hq * 4 + k4
                    OP("tensor", lambda e, head=head, k4=k4: e.transpose(
                        P[6][:, k4 * 128:(k4 + 1) * 128], acc[:, head * 128:(head + 1) * 128],
                        identF[:]),
                       waits=[(c_ve, c_ve.n), (c_se, c_se.n)] if k4 == 0 else [],
                       incs=[c_pe] if k4 == 3 else [])
                OP("scalar", lambda e, hq=hq, qip=qip: e.copy(
                    out=combT[:, qip * 4096 + hq * 512:qip * 4096 + (hq + 1) * 512], in_=P[6][:]),
                   waits=[(c_pe, c_pe.n)], incs=[c_se])
            QT_PE = c_pe.n

            if qip == 3:
                g0 = grp * 512
                for oc in range(8):
                    for f in range(HQ):
                        bsel = f % 2
                        wt = wtl[bsel]
                        OP("sync", lambda e, f=f, oc=oc, wt=wt: e.dma_start(
                            out=wt, in_=stg["w_oT_bf"][f * 128:(f + 1) * 128, oc * 512:(oc + 1) * 512]),
                           waits=[(c_wk, wdone["w_oT"] if (grp == 0 and oc == 0 and f < 2) else 0),
                                  (c_pe, wtl_last[bsel]),
                                  (c_se, c_se.n if (oc == 0 and f == 0) else 0)],
                           incs=[c_wl2[bsel]])
                        for qp in range(4):
                            OP("tensor", lambda e, f=f, qp=qp, wt=wt: e.matmul(
                                P[qp][:], combT[:, qp * 4096 + f * 128:qp * 4096 + (f + 1) * 128], wt,
                                start=(f == 0), stop=(f == HQ - 1)),
                               waits=([(c_wl2[bsel], c_wl2[bsel].n)] +
                                      ([(c_ve, c_ve.n)] if f == 0 else [])) if qp == 0 else [],
                               incs=[c_pe] if qp == 3 else [])
                        wtl_last[bsel] = c_pe.n
                    WO_OK = c_pe.n
                    for qp in range(4):
                        bx = qp % 2
                        OP("sync", lambda e, qp=qp, oc=oc, g0=g0, bx=bx: e.dma_start(
                            out=xres[bx], in_=x_own[g0 + qp * 128:g0 + (qp + 1) * 128,
                                                   oc * 512:(oc + 1) * 512]),
                           waits=[(c_ve, xres_last[bx])], incs=[c_x2[bx]])
                        OP("vector", lambda e, qp=qp, bx=bx: e.tensor_tensor(
                            out=mtmp[bx], in0=P[qp][:], in1=xres[bx], op=ALU.add),
                           waits=[(c_x2[bx], c_x2[bx].n), (c_pe, WO_OK),
                                  (c_mt2[bx], mt_last[bx])], incs=[c_ve])
                        xres_last[bx] = c_ve.n
                        OP("vector", lambda e, qp=qp, oc=oc, bx=bx: e.bn_stats(
                            out=bn48[:, qp * 48 + oc * 6:qp * 48 + oc * 6 + 6], in_=mtmp[bx]),
                           waits=[(c_ve, c_ve.n)], incs=[c_ve])
                        OP("scalar", lambda e, qp=qp, oc=oc, g0=g0, bx=bx: e.dma_start(
                            out=mid_d[g0 + qp * 128:g0 + (qp + 1) * 128, oc * 512:(oc + 1) * 512],
                            in_=mtmp[bx]),
                           waits=[(c_ve, c_ve.n)], incs=[c_mt2[bx]])
                        mt_last[bx] = c_mt2[bx].n
                MID_OK = (c_mt2[0].n, c_mt2[1].n)
                for qp in range(4):
                    OP("sync", lambda e, qp=qp, g0=g0: e.dma_start(
                        out=xmid, in_=mid_d[g0 + qp * 128:g0 + (qp + 1) * 128, :]),
                       waits=[(c_mt2[0], MID_OK[0]), (c_mt2[1], MID_OK[1]),
                              (c_ve, c_ve.n)], incs=[c_xm])
                    OP("vector", lambda e, qp=qp: e.bn_aggr(out=mv2[:], in_=bn48[:, qp * 48:(qp + 1) * 48]),
                       waits=[(c_xm, c_xm.n)], incs=[c_ve])
                    OP("vector", lambda e: e.tensor_scalar(out=stdv[:], in0=mv2[:, 1:2], scalar1=EPS,
                                                           scalar2=None, op0=ALU.add, op1=ALU.bypass),
                       waits=[(c_ve, c_ve.n)], incs=[c_ve])
                    OP("scalar", lambda e: e.activation(out=stdv[:], in_=stdv[:], func=AF.Sqrt),
                       waits=[(c_ve, c_ve.n)], incs=[c_se])
                    OP("vector", lambda e: e.reciprocal(out=rstd[:], in_=stdv[:]),
                       waits=[(c_se, c_se.n)], incs=[c_ve])
                    OP("vector", lambda e: e.tensor_scalar(out=ln2tl, in0=xmid, scalar1=mv2[:, 0:1],
                                                           scalar2=rstd[:], op0=ALU.subtract,
                                                           op1=ALU.mult),
                       waits=[(c_ve, c_ve.n), (c_l2, l2_last[0])], incs=[c_ve])
                    OP("scalar", lambda e, qp=qp, g0=g0: e.dma_start(
                        out=ln2_d[g0 + qp * 128:g0 + (qp + 1) * 128, :], in_=ln2tl),
                       waits=[(c_ve, c_ve.n)], incs=[c_l2])
                    l2_last[0] = c_l2.n
                LN2_OK = c_l2.n
                for f in range(32):
                    OP("sync", lambda e, f=f, g0=g0: e.dma_start(
                        out=lnT2[f], in_=ln2_d[g0:g0 + 512, f * 128:(f + 1) * 128], transpose=True),
                       waits=[(c_l2, LN2_OK), (c_pe, c_pe.n)] if f == 0 else [], incs=[c_lt])
                LT2 = c_lt.n
                for qp in range(4):
                    for f in range(32):
                        bsel = f % 2
                        lg = lgt[bsel]
                        OP("sync", lambda e, f=f, lg=lg: e.dma_start(
                            out=lg[:, 0:252], in_=stg["lgAT_bf"][f * 128:(f + 1) * 128, :]),
                           waits=[(c_wk, wdone["lgAT"] if (grp == 0 and qp == 0 and f < 2) else 0),
                                  (c_pe, lgt_last[bsel]), (c_lt, LT2 if f == 0 else 0)],
                           incs=[c_wl2[bsel]])
                        OP("tensor", lambda e, f=f, qp=qp, lg=lg: e.matmul(
                            P[4][:, 0:252], lnT2[f][:, qp * 128:(qp + 1) * 128], lg[:, 0:252],
                            start=(f == 0), stop=(f == 31)),
                           waits=[(c_wl2[bsel], c_wl2[bsel].n),
                                  (c_se, c_se.n if f == 0 else 0),
                                  (c_ve, c_ve.n if f == 0 else 0)],
                           incs=[c_pe])
                        lgt_last[bsel] = c_pe.n
                    HL_OK = c_pe.n
                    OP("vector", lambda e: e.max(out=top8[:], in_=P[4][:, 224:252]),
                       waits=[(c_pe, HL_OK), (c_ve, c_ve.n)], incs=[c_ve])
                    OP("vector", lambda e: e.max_index(out=mi8[:], in_max=top8[:],
                                                       in_values=P[4][:, 224:252]),
                       waits=[(c_ve, c_ve.n)], incs=[c_ve])
                    OP("vector", lambda e: e.tensor_copy(out=i0f[:], in_=mi8[:, 0:1]),
                       waits=[(c_ve, c_ve.n)], incs=[c_ve])
                    OP("vector", lambda e: e.tensor_copy(out=i1f[:], in_=mi8[:, 1:2]), incs=[c_ve])
                    OP("vector", lambda e: e.tensor_scalar(out=vdif[:], in0=top8[:, 0:1],
                                                           scalar1=top8[:, 1:2], scalar2=None,
                                                           op0=ALU.subtract, op1=ALU.bypass),
                       waits=[(c_ve, c_ve.n)], incs=[c_ve])
                    OP("scalar", lambda e: e.activation(out=p1c[:], in_=vdif[:], func=AF.Sigmoid),
                       waits=[(c_ve, c_ve.n)], incs=[c_se])
                    OP("vector", lambda e: e.tensor_scalar(out=p2c[:], in0=p1c[:], scalar1=-1.0,
                                                           scalar2=1.0, op0=ALU.mult, op1=ALU.add),
                       waits=[(c_se, c_se.n)], incs=[c_ve])
                    OP("vector", lambda e: e.tensor_scalar(out=gfull[:], in0=iota28[:], scalar1=i0f[:],
                                                           scalar2=p1c[:], op0=ALU.is_equal,
                                                           op1=ALU.mult),
                       waits=[(c_ve, c_ve.n)], incs=[c_ve])
                    OP("vector", lambda e: e.tensor_scalar(out=gf2[:], in0=iota28[:], scalar1=i1f[:],
                                                           scalar2=p2c[:], op0=ALU.is_equal,
                                                           op1=ALU.mult), incs=[c_ve])
                    OP("vector", lambda e: e.tensor_tensor(out=gfull[:], in0=gfull[:], in1=gf2[:],
                                                           op=ALU.add),
                       waits=[(c_ve, c_ve.n)], incs=[c_ve])
                    OP("vector", lambda e: e.tensor_copy(
                        out=gexp[:].rearrange("p (a b) -> p a b", b=8),
                        in_=gfull[:].rearrange("p (a b) -> p a b", b=1).broadcast_to((128, 28, 8))),
                       waits=[(c_ve, c_ve.n)], incs=[c_ve])
                    OP("vector", lambda e: e.scalar_tensor_tensor(out=hgq[:], in0=P[4][:, 0:224],
                                                                  scalar=2.0, in1=gexp[:],
                                                                  op0=ALU.mult, op1=ALU.mult),
                       waits=[(c_ve, c_ve.n)], incs=[c_ve])
                    HG_OK = c_ve.n
                    OP("tensor", lambda e: e.transpose(PTb[:, 0:128], hgq[:, 0:128], ident[:]),
                       waits=[(c_ve, HG_OK), (c_se, c_se.n)], incs=[])
                    OP("tensor", lambda e: e.transpose(PTb[0:96, 128:256], hgq[:, 128:224], ident[:]),
                       incs=[c_pe])
                    OP("scalar", lambda e, qp=qp: e.copy(out=hgt4[qp][:, 0:128], in_=PTb[:, 0:128]),
                       waits=[(c_pe, c_pe.n)], incs=[c_se])
                    OP("scalar", lambda e, qp=qp: e.copy(out=hgt4[qp][0:96, 128:256],
                                                         in_=PTb[0:96, 128:256]), incs=[c_se])
                HGT_OK = c_se.n
                for oc in range(8):
                    for f in range(32):
                        bsel = f % 2
                        wt = wtl[bsel]
                        OP("sync", lambda e, f=f, oc=oc, wt=wt: e.dma_start(
                            out=wt, in_=stg["bWT_bf"][f * 128:(f + 1) * 128, oc * 512:(oc + 1) * 512]),
                           waits=[(c_wk, wdone["bWT"] if (grp == 0 and oc == 0 and f < 2) else 0),
                                  (c_pe, wtl_last[bsel]),
                                  (c_se, HGT_OK if (oc == 0 and f == 0) else 0)],
                           incs=[c_wl2[bsel]])
                        for qp in range(4):
                            OP("tensor", lambda e, f=f, qp=qp, wt=wt: e.matmul(
                                P[qp][:], lnT2[f][:, qp * 128:(qp + 1) * 128], wt,
                                start=(f == 0), stop=False),
                               waits=([(c_wl2[bsel], c_wl2[bsel].n)] +
                                      ([(c_ve, c_ve.n)] if f == 0 else [])) if qp == 0 else [],
                               incs=[c_pe] if qp == 3 else [])
                        wtl_last[bsel] = c_pe.n
                    OP("sync", lambda e, oc=oc: e.dma_start(
                        out=bft[0], in_=stg["bflat_bf"][0:128, oc * 512:(oc + 1) * 512]),
                       waits=[(c_wk, wdone["bflat"] if (grp == 0 and oc == 0) else 0),
                              (c_pe, c_pe.n)], incs=[c_bf])
                    OP("sync", lambda e, oc=oc: e.dma_start(
                        out=bft[1][0:96, :], in_=stg["bflat_bf"][128:224, oc * 512:(oc + 1) * 512]),
                       incs=[c_bf])
                    BF_OK = c_bf.n
                    for qp in range(4):
                        OP("tensor", lambda e, qp=qp: e.matmul(
                            P[qp][:], hgt4[qp][:, 0:128], bft[0], start=False, stop=False),
                           waits=[(c_bf, BF_OK)] if qp == 0 else [], incs=[])
                        OP("tensor", lambda e, qp=qp: e.matmul(
                            P[qp][:], hgt4[qp][0:96, 128:256], bft[1][0:96, :],
                            start=False, stop=True),
                           incs=[c_pe] if qp == 3 else [])
                    FF_OK = c_pe.n
                    for qp in range(4):
                        bx = qp % 2
                        OP("sync", lambda e, qp=qp, oc=oc, g0=g0, bx=bx: e.dma_start(
                            out=xres[bx], in_=mid_d[g0 + qp * 128:g0 + (qp + 1) * 128,
                                                    oc * 512:(oc + 1) * 512]),
                           waits=[(c_ve, xres_last[bx])], incs=[c_x2[bx]])
                        OP("vector", lambda e, qp=qp, bx=bx: e.tensor_tensor(
                            out=outt[bx], in0=P[qp][:], in1=xres[bx], op=ALU.add),
                           waits=[(c_x2[bx], c_x2[bx].n), (c_pe, FF_OK),
                                  (c_ot2[bx], ot_last[bx])],
                           incs=[c_ve])
                        xres_last[bx] = c_ve.n
                        OP("scalar", lambda e, qp=qp, oc=oc, g0=g0, bx=bx: e.dma_start(
                            out=out_d[g0 + qp * 128:g0 + (qp + 1) * 128, oc * 512:(oc + 1) * 512],
                            in_=outt[bx]),
                           waits=[(c_ve, c_ve.n)], incs=[c_ot2[bx]])
                        ot_last[bx] = c_ot2[bx].n

    OP("vector", lambda e: e.nop(), waits=[(c_ot2[0], c_ot2[0].n), (c_ot2[1], c_ot2[1].n)])
    _syncch = [c_ld, c_lt, c_wl2[0], c_wl2[1], c_bf, c_w2, c_x2[0], c_x2[1], c_xm]
    _scalch = [c_ln, c_bq, c_brv, c_bk, c_st, c_mt2[0], c_mt2[1], c_l2, c_ot2[0], c_ot2[1]]
    _gpch = [c_wc2[0], c_wc2[1], c_sc, c_gw, c_gf2[0], c_gf2[1]]
    OP("sync", lambda e: e.nop(), waits=[(ch, ch.n) for ch in _syncch])
    OP("scalar", lambda e: e.nop(), waits=[(ch, ch.n) for ch in _scalch])
    OP("gpsimd", lambda e: e.nop(), waits=[(ch, ch.n) for ch in _gpch])

    with nc.Block() as block:
        Pg.emit(block)
    nc._pg = Pg
    return nc


def _host_feed(core, inputs):
    b, h = core // 2, core % 2
    x = inputs["L0_attn_hidden_states"]
    feed = {
        "x_full": np.ascontiguousarray(x[b]),
        "x_own": np.ascontiguousarray(x[b, h * OWN:(h + 1) * OWN]),
        "wqkvT": np.ascontiguousarray(inputs["L0_attn_w_qkv"].T),
        "w_oT": np.ascontiguousarray(inputs["L0_attn_w_o"].T),
        "bWT": np.ascontiguousarray(inputs["L0_ffn_base_W"].T),
        "k_w1T": np.ascontiguousarray(inputs["L0_attn_k_w1"].T),
        "v_w1T": np.ascontiguousarray(inputs["L0_attn_v_w1"].T),
        "k_w2T": np.ascontiguousarray(inputs["L0_attn_k_w2"].T),
        "v_w2T": np.ascontiguousarray(inputs["L0_attn_v_w2"].T),
        "wgT": np.ascontiguousarray(inputs["L0_attn_w_gate"].T),
        "lgAT": np.ascontiguousarray(
            np.concatenate([inputs["L0_ffn_lora_A"].reshape(224, D).T,
                            inputs["L0_ffn_gate_W"].T], axis=1)),
        "bflat": np.ascontiguousarray(
            inputs["L0_ffn_lora_B"].transpose(0, 2, 1).reshape(224, D)),
        "kb1": np.ascontiguousarray(inputs["L0_attn_k_b1"].reshape(16, 128, 1)),
        "vb1": np.ascontiguousarray(inputs["L0_attn_v_b1"].reshape(16, 128, 1)),
        "kb2": np.ascontiguousarray(inputs["L0_attn_k_b2"].reshape(128, 1)),
        "vb2bc": np.ascontiguousarray(np.broadcast_to(inputs["L0_attn_v_b2"], (128, 128))),
        "bgbc": np.ascontiguousarray(np.broadcast_to(inputs["L0_attn_b_gate"], (128, 96))),
        "kposT": np.ascontiguousarray(inputs["L0_attn_k_pos"].transpose(0, 2, 1)),
        "vposT": np.ascontiguousarray(inputs["L0_attn_v_pos"].transpose(0, 2, 1)),
        "memk": np.ascontiguousarray(inputs["L0_attn_mem_kv"][0].transpose(0, 2, 1)),
        "memv": np.ascontiguousarray(inputs["L0_attn_mem_kv"][1]),
        "tb_identF": np.eye(128, dtype=np.float32),
    }
    feed.update(_HT)
    feed.update(_host_tables_core(core))
    return feed


def kernel(**inputs):
    nc = build()
    in_maps = [_host_feed(c, inputs) for c in range(8)]
    res = run_bass_kernel_spmd(nc, in_maps, core_ids=list(range(8)))
    out = np.empty((4, S, D), np.float32)
    for c in range(8):
        b, h = c // 2, c % 2
        out[b, h * OWN:(h + 1) * OWN] = res.results[c]["out"]
    return out
